# version 07
# baseline (speedup 1.0000x reference)
import math
import numpy as np
from contextlib import ExitStack

import concourse.bass as bass
import concourse.mybir as mybir
from concourse.bass_utils import run_bass_kernel_spmd

F32 = mybir.dt.float32
I32 = mybir.dt.int32
AF = mybir.ActivationFunctionType
ALU = mybir.AluOpType
AX = mybir.AxisListType

D = 1024
S = 4096
HALF = 2048
NCH = 8
NE = 16
SHIFT = 16.0
TWO_PI = 2.0 * math.pi
PI = math.pi

ENGS = ("te", "ve", "se", "sy", "gp")
DMA_STREAMS = ("w0", "w1", "x0", "k0", "k1", "v0", "v1", "q0", "q1",
               "e0", "e1", "st", "cf", "pos", "ini")
ALL_SEMS = ("te", "ve", "se") + DMA_STREAMS
CONST_VALS = [0.0, 1.0, -1.0, 0.125, -SHIFT, 1e-5, -0.5]


def _hl(x):
    hi = np.float32(x)
    lo = np.float32(np.float64(x) - np.float64(hi))
    return float(hi), float(lo)


C_LNTH = _hl(-math.log(10000.0) / 32.0)
C_2PI = _hl(TWO_PI)


class MK:
    def __init__(self):
        self.nc = bass.Bass("TRN2", target_bir_lowering=False)
        self.ops = []
        self.tick = {e: 0 for e in ALL_SEMS}
        self.water = {}
        self.banktok = [None] * 8

    def op(self, eng, fn, deps=(), inc=True, hard_deps=()):
        waits = []
        agg = {}
        for d in deps:
            if d is None:
                continue
            e2, v = d
            if e2 == eng and e2 == "te":
                continue
            agg[e2] = max(agg.get(e2, 0), v)
        for d in hard_deps:
            if d is None:
                continue
            e2, v = d
            agg[e2] = max(agg.get(e2, 0), v)
        for e2, v in agg.items():
            key = (eng, e2)
            if v > self.water.get(key, -1):
                waits.append((e2, v))
                self.water[key] = v
        tok = None
        n = 0
        if inc:
            n = 1
            self.tick[eng] += n
            tok = (eng, self.tick[eng])
        self.ops.append((eng, fn, waits, (eng, n) if n else None))
        return tok

    def dma(self, q, out_ap, in_ap, deps=(), stream="st"):
        waits = []
        agg = {}
        for d in deps:
            if d is None:
                continue
            e2, v = d
            agg[e2] = max(agg.get(e2, 0), v)
        for e2, v in agg.items():
            key = (q, e2)
            if v > self.water.get(key, -1):
                waits.append((e2, v))
                self.water[key] = v
        self.tick[stream] += 16
        tok = (stream, self.tick[stream])
        self.ops.append((q, lambda E, o=out_ap, i=in_ap: E.dma_start(o, i),
                         waits, (stream, 16)))
        return tok

    def barrier(self):
        snap = dict(self.tick)
        for e in ENGS:
            deps = [(e2, v) for e2, v in snap.items() if v > 0 and e2 != e]
            self.op(e, None, deps, inc=False)


def build_program(debug=False):
    mk = MK()
    nc = mk.nc

    def din(name, shape, dt=F32):
        return nc.dram_tensor(name, shape, dt, kind="ExternalInput").ap()

    x_fm = din("x_fm", [D, S])
    wqkv_a = din("wqkv_a", [D, 3 * D])
    wqkv_b = din("wqkv_b", [D, 3 * D])
    wo_d = din("wo", [D, D])
    gwt = din("gwt", [D, NE])
    gb = din("gb", [1, NE])
    w13t = din("w13t", [NE + 1, D, D])
    w2t = din("w2t", [NE + 1, 512, D])
    masks_d = din("masks", [128, 2048])
    ident_d = din("ident", [128, 128])
    prot_d = din("prot", [128, 128])
    irep_d = din("irep", [1, 128])
    posf_d = din("pos_full", [1, S])
    posq_d = din("pos_q", [1, HALF])
    hcv_d = din("hcv", [128, 1])

    y_out = nc.dram_tensor("y", [D, HALF], F32, kind="ExternalOutput").ap()

    def scr(name, shape):
        return nc.dram_tensor(name, shape, F32, kind="Internal").ap()

    q_scr = scr("q_scr", [D, S])
    k_scr = scr("k_scr", [D, S])
    vt_scr = scr("vt_scr", [S, D])
    ao_scr = scr("ao_scr", [D, S])
    x2_scr = scr("x2_scr", [D, S])
    x3_scr = scr("x3_scr", [D, HALF])

    dbg = {}
    if debug:
        for nm, shape in [("d_q", [D, S]), ("d_k", [D, S]), ("d_vt", [S, D]),
                          ("d_ao", [D, S]), ("d_x2", [D, S]), ("d_k2", [D, S]),
                          ("d_vt2", [S, D]), ("d_q2", [D, S]), ("d_ao2", [D, S]),
                          ("d_x3", [D, HALF]),
                          ("d_xn", [128, 4096]), ("d_cw", [128, 512]),
                          ("d_lg", [128, 64]),
                          ("d_cf", [1, 4096]), ("d_h", [128, 2048])]:
            dbg[nm] = nc.dram_tensor(nm, shape, F32, kind="ExternalOutput").ap()

    with ExitStack() as ctx:
        def sbt(name, shape, dt=F32):
            return ctx.enter_context(nc.sbuf_tensor(name, shape, dt))

        wbuf = sbt("wbuf", [128, 24576])
        xbuf = sbt("xbuf", [128, 4096])
        xnbuf = sbt("xnbuf", [128, 4096])
        hbuf = sbt("hbuf", [128, 2048])
        cbuf = sbt("cbuf", [128, 512])
        mrb = sbt("mrb", [128, 512])
        maskb = sbt("maskb", [128, 2048])
        identb = sbt("identb", [128, 128])
        onesb = sbt("onesb", [128, 128])
        protb = sbt("protb", [128, 128])
        gwtb = sbt("gwtb", [128, 144])
        statb = sbt("statb", [128, 1024])
        mbc = sbt("mbc", [128, 512])
        rbc = sbt("rbc", [128, 512])
        cwork = sbt("cwork", [128, 512])
        cflat = sbt("cflat", [1, 4096])
        ropei = sbt("ropei", [128, 512], I32)
        constb = sbt("constb", [128, 448])
        hcvb = sbt("hcvb", [128, 1])

        for i, v in enumerate(CONST_VALS):
            nc.const_aps.register(F32, float(v), constb[:, i:i + 1])

        psum = [ctx.enter_context(nc.psum_tensor(f"pb{i}", [128, 512], F32))
                for i in range(8)]
        sems = {e: ctx.enter_context(nc.semaphore(f"sem_{e}")) for e in ALL_SEMS}
        block = ctx.enter_context(nc.Block())

        irepb = constb[0:1, 16:144]
        iftmp = constb[0:1, 144:272]
        ifrow = constb[0:1, 272:400]
        gbsb = constb[0:1, 400:416]
        posb = statb[0:1, 512:1024]
        gbb = gwtb[:, 128:144]

        def ln_tile(xdep, extra_dep=None):
            tsum = None
            for k in range(NCH):
                xc = xbuf[:, k * 512:(k + 1) * 512]
                tsum = mk.op("te", lambda E, i=xc, k=k: E.matmul(
                    psum[6][0:1, :], onesb[:, 0:1], i,
                    start=(k == 0), stop=(k == NCH - 1)),
                    deps=([xdep, extra_dep, mk.banktok[6]] if k == 0 else []),
                    inc=(k == NCH - 1))
            sqtoks = []
            for k in range(NCH):
                xc = xbuf[:, k * 512:(k + 1) * 512]
                sq = xnbuf[:, k * 512:(k + 1) * 512]
                sqtoks.append(mk.op("se", lambda E, o=sq, i=xc: E.activation(
                    o, i, AF.Square), deps=[xdep, extra_dep]))
            tsq = None
            for k in range(NCH):
                sq = xnbuf[:, k * 512:(k + 1) * 512]
                tsq = mk.op("te", lambda E, i=sq, k=k: E.matmul(
                    psum[7][0:1, :], onesb[:, 0:1], i,
                    start=(k == 0), stop=(k == NCH - 1)),
                    deps=([sqtoks[k], mk.banktok[7]] if k == 0 else [sqtoks[k]]),
                    inc=(k == NCH - 1))
            mean = statb[0:1, 0:512]
            m2 = statb[0:1, 512:1024]
            t1 = mk.op("ve", lambda E: E.tensor_scalar_mul(
                mean, psum[6][0:1, :], 1.0 / D), deps=[tsum])
            tbm = mk.op("te", lambda E: E.matmul(
                psum[4][:], onesb[0:1, 0:128], mean, start=True, stop=True),
                deps=[t1, mk.banktok[4]])
            tmb = mk.op("ve", lambda E: E.tensor_copy(mbc[:], psum[4][:]), deps=[tbm])
            mk.banktok[4] = tmb
            t2 = mk.op("ve", lambda E: E.tensor_mul(m2, mean, mean), deps=[t1])
            tvar = mk.op("ve", lambda E: E.scalar_tensor_tensor(
                mean, psum[7][0:1, :], 1.0 / D, m2, ALU.mult, ALU.subtract),
                deps=[tsq, t2, tbm])
            mk.banktok[6] = t1
            mk.banktok[7] = tvar
            tln = mk.op("se", lambda E: E.activation(m2, mean, AF.Ln, bias=1e-5),
                        deps=[tvar])
            trs = mk.op("se", lambda E: E.activation(mean, m2, AF.Exp, scale=-0.5),
                        deps=[tln])
            tbr = mk.op("te", lambda E: E.matmul(
                psum[5][:], onesb[0:1, 0:128], mean, start=True, stop=True),
                deps=[trs, mk.banktok[5]])
            trb = mk.op("ve", lambda E: E.tensor_copy(rbc[:], psum[5][:]), deps=[tbr])
            mk.banktok[5] = trb
            tmr = mk.op("ve", lambda E: E.tensor_mul(mrb[:], mbc[:], rbc[:]),
                        deps=[tmb, trb])
            xt = None
            for k in range(NCH):
                xc = xbuf[:, k * 512:(k + 1) * 512]
                xn = xnbuf[:, k * 512:(k + 1) * 512]
                tm = mk.op("ve", lambda E, o=xn, i=xc: E.tensor_mul(o, i, rbc[:]),
                           deps=[trb, tvar])
                xt = mk.op("ve", lambda E, o=xn: E.tensor_sub(o, o, mrb[:]),
                           deps=[tmr, tm])
            return xt, tbr

        def rope_tables(pos_src, pos_off, dep):
            tp = mk.dma("sy", posb, pos_src[0:1, pos_off:pos_off + 512], deps=[dep],
                        stream="pos")
            targ = mk.op("te", lambda E: E.matmul(
                psum[6][:], ifrow, posb, start=True, stop=True),
                deps=[tp, mk.banktok[6]])
            w1 = hbuf[:, 1024:1536]
            w2 = hbuf[:, 1536:2048]
            w3 = mrb[:, 0:512]
            toks = []
            last = None
            for which, tbl, extra in (("sin", rbc, 0.5), ("cos", mbc, 0.75)):
                t0 = mk.op("ve", lambda E, x=extra: E.tensor_scalar(
                    w1, psum[6][:], 1.0 / TWO_PI, x, ALU.mult, ALU.add),
                    deps=[targ, last])
                tc1 = mk.op("ve", lambda E: E.tensor_copy(ropei[:], w1), deps=[t0])
                tc2 = mk.op("ve", lambda E: E.tensor_copy(w1, ropei[:]), deps=[tc1])
                t = mk.op("ve", lambda E: E.scalar_tensor_tensor(
                    w2, w1, C_2PI[0], psum[6][:], ALU.mult, ALU.subtract), deps=[tc2])
                t = mk.op("ve", lambda E: E.scalar_tensor_tensor(
                    w3, w1, C_2PI[1], w2, ALU.mult, ALU.add), deps=[t])
                cur, alt = w3, w2
                if which == "cos":
                    t = mk.op("ve", lambda E, o=alt, i=cur: E.tensor_scalar_add(
                        o, i, -PI / 2.0), deps=[t])
                    cur, alt = alt, cur
                t = mk.op("ve", lambda E, i=cur: E.tensor_single_scalar(
                    w1, i, PI, ALU.is_gt), deps=[t])
                t = mk.op("ve", lambda E, o=alt, i=cur: E.scalar_tensor_tensor(
                    o, w1, -TWO_PI, i, ALU.mult, ALU.add), deps=[t])
                cur, alt = alt, cur
                t = mk.op("ve", lambda E, i=cur: E.tensor_single_scalar(
                    w1, i, -PI, ALU.is_lt), deps=[t])
                t = mk.op("ve", lambda E, o=alt, i=cur: E.scalar_tensor_tensor(
                    o, w1, TWO_PI, i, ALU.mult, ALU.add), deps=[t])
                cur, alt = alt, cur
                ts = mk.op("se", lambda E, o=tbl, i=cur: E.activation(
                    o[:], i, AF.Sin, scale=-1.0), deps=[t])
                last = t
                toks.append(ts)
            mk.banktok[6] = toks[-1]
            return toks[0], toks[1]

        for i, v in enumerate(CONST_VALS):
            mk.op("ve", lambda E, i=i, v=v: E.memset(constb[:, i:i + 1], float(v)),
                  inc=False)
        t_ones = mk.op("ve", lambda E: E.memset(onesb[:], 1.0))
        mk.dma("sy", identb[:], ident_d, stream="ini")
        mk.dma("sy", protb[:], prot_d, stream="ini")
        mk.dma("sy", maskb[:], masks_d, stream="ini")
        mk.dma("sy", gwtb[:, 0:128].rearrange("p (k e) -> p k e", k=8),
               gwt.rearrange("(k p) e -> p k e", p=128), stream="ini")
        mk.dma("sy", irepb, irep_d, stream="ini")
        mk.dma("sy", gbsb, gb, stream="ini")
        t_ini = mk.dma("sy", hcvb[:], hcv_d, stream="ini")
        t_ir = t_gb = t_ini
        t_bb = mk.op("te", lambda E: E.matmul(
            psum[6][:, 0:16], onesb[0:1, 0:128], gbsb, start=True, stop=True),
            deps=[t_gb, t_ones])
        tgbb = mk.op("ve", lambda E: E.tensor_copy(gbb, psum[6][:, 0:16]), deps=[t_bb])
        mk.banktok[6] = tgbb
        ti1 = mk.op("ve", lambda E: E.tensor_scalar_mul(iftmp, irepb, C_LNTH[0]),
                    deps=[t_ir])
        ti2 = mk.op("ve", lambda E: E.scalar_tensor_tensor(
            ifrow, irepb, C_LNTH[1], iftmp, ALU.mult, ALU.add), deps=[ti1])
        mk.op("se", lambda E: E.activation(ifrow, ifrow, AF.Exp), deps=[ti2])
        mk.barrier()

        def load_x_from(src):
            def f(tt):
                return mk.dma("sy", xbuf[:].rearrange("p (k t) -> p k t", k=8),
                              src.rearrange("(k p) t -> p k t", p=128)
                              [:, :, tt * 512:(tt + 1) * 512], stream="x0")
            return f

        def load_x2my(tt):
            tA = mk.dma("sy", xbuf[:].rearrange("p (k t) -> p k t", k=8),
                        x2_scr.rearrange("(k p) t -> p k t", p=128)
                        [:, :, tt * 512:(tt + 1) * 512], stream="x0")
            tB = mk.dma("sy", xnbuf[:].rearrange("p (k t) -> p k t", k=8),
                        x2_scr.rearrange("(k p) t -> p k t", p=128)
                        [:, :, HALF + tt * 512:HALF + tt * 512 + 512], stream="x0")
            tok = None
            for k in range(NCH):
                xa = xbuf[:, k * 512:(k + 1) * 512]
                xb = xnbuf[:, k * 512:(k + 1) * 512]
                t1 = mk.op("ve", lambda E, a=xa, b=xb: E.tensor_sub(b, b, a),
                           deps=[tA, tB])
                t2 = mk.op("ve", lambda E, b=xb: E.tensor_scalar_mul(
                    mrb[:], b, hcvb[:, 0:1]), deps=[t1])
                tok = mk.op("ve", lambda E, a=xa: E.tensor_add(a, a, mrb[:]),
                            deps=[t2])
            return tok

        def qkv_pass(w_dram, x_src_fn, ttiles, mspec, do_v, rope_pos):
            wt = None
            for k in range(8):
                wt = mk.dma("sy", wbuf[:, k * 3072:(k + 1) * 3072],
                            w_dram[k * 128:(k + 1) * 128, :], stream="w0")
            ev = [cbuf[:, 0:512], hbuf[:, 0:512]]
            rot = hbuf[:, 512:1024]
            rottok = None
            pp = 0
            for tt in ttiles:
                xdep = x_src_fn(tt)
                xntok, statfree = ln_tile(xdep, extra_dep=wt)
                tsin = tcos = None
                if rope_pos is not None:
                    tsin, tcos = rope_tables(rope_pos[0], rope_pos[1] + tt * 512,
                                             statfree)
                evtok = [None, None]
                evi = 0
                for (m, dst) in mspec:
                    pb_i = pp % 4
                    pb = psum[pb_i]
                    last = None
                    for k in range(8):
                        lhs = wbuf[:, k * 3072 + m * 128: k * 3072 + m * 128 + 128]
                        rhs = xnbuf[:, k * 512:(k + 1) * 512]
                        last = mk.op("te", lambda E, o=pb, l=lhs, r=rhs, k=k:
                                     E.matmul(o[:], l, r, start=(k == 0), stop=(k == 7)),
                                     deps=([xntok, evtok[evi], mk.banktok[pb_i]]
                                           if k == 0 else []),
                                     inc=(k == 7))
                    e = ev[evi]
                    tev = mk.op("ve", lambda E, o=e, i=pb: E.tensor_copy(o, i[:]),
                                deps=[last, evtok[evi]])
                    mk.banktok[pb_i] = tev
                    if rope_pos is not None:
                        rb_i = 4 + (pp % 2)
                        prr = psum[rb_i]
                        trm = mk.op("te", lambda E, o=prr, i=e: E.matmul(
                            o[:], protb[:], i, start=True, stop=True),
                            deps=[tev, mk.banktok[rb_i]])
                        tt1 = mk.op("ve", lambda E, o=e: E.tensor_mul(o, o, mbc[:]),
                                    deps=[tcos, trm])
                        tt2 = mk.op("ve", lambda E, i=prr: E.tensor_mul(
                            rot, i[:], rbc[:]), deps=[trm, tsin, rottok])
                        mk.banktok[rb_i] = tt2
                        tev = mk.op("ve", lambda E, o=e: E.tensor_add(o, o, rot),
                                    deps=[tt1, tt2])
                        rottok = tev
                    hp = m % 8
                    tst = mk.dma("gp", dst[hp * 128:(hp + 1) * 128,
                                           tt * 512:tt * 512 + 512], e, deps=[tev],
                                 stream=f"e{evi}")
                    evtok[evi] = tst
                    evi ^= 1
                    pp += 1
                if do_v:
                    for tc in range(4):
                        for n in range(2):
                            pb_i = 4 + ((tc * 2 + n) % 2)
                            pb = psum[pb_i]
                            last = None
                            for k in range(8):
                                lhs = xnbuf[:, k * 512 + tc * 128:
                                            k * 512 + tc * 128 + 128]
                                rhs = wbuf[:, k * 3072 + 2048 + n * 512:
                                           k * 3072 + 2048 + n * 512 + 512]
                                last = mk.op("te", lambda E, o=pb, l=lhs, r=rhs, k=k:
                                             E.matmul(o[:], l, r, start=(k == 0),
                                                      stop=(k == 7)),
                                             deps=([xntok, evtok[evi],
                                                    mk.banktok[pb_i]]
                                                   if k == 0 else []),
                                             inc=(k == 7))
                            e = ev[evi]
                            tev = mk.op("ve", lambda E, o=e, i=pb: E.tensor_copy(
                                o, i[:]), deps=[last, evtok[evi]])
                            mk.banktok[pb_i] = tev
                            r0 = tt * 512 + tc * 128
                            tst = mk.dma("gp", vt_scr[r0:r0 + 128,
                                                      n * 512:n * 512 + 512], e,
                                         deps=[tev], stream=f"e{evi}")
                            evtok[evi] = tst
                            evi ^= 1
                mk.barrier()

        def attn_pass(nqs, causal):
            kt = [wbuf[:, 0:1024], wbuf[:, 1024:2048]]
            vt_base = [2048, 3088]
            vt = [wbuf[:, b:b + 1040] for b in vt_base]
            vt4 = [wbuf[:, b:b + 1040].rearrange("p (j h c) -> p j h c", j=8, h=2)
                   for b in vt_base]
            qt = [wbuf[:, 4128:4640], wbuf[:, 4640:5152]]
            et = [[wbuf[:, 5152 + (h * 2 + i) * 512: 5152 + (h * 2 + i) * 512 + 512]
                   for i in range(2)] for h in range(2)]
            ao = [wbuf[:, 7200:7712], wbuf[:, 7712:8224]]
            lb = [hbuf[:, 0:512], hbuf[:, 512:1024]]
            vsrc = vt_scr.rearrange("(J p) (H c) -> p J H c", p=128, c=64)
            tve = [None, None]
            for i in range(2):
                tve[i] = mk.op("ve", lambda E, i=i: E.memset(
                    vt4[i][:, :, :, 64:65], 1.0))
            kcons = [None, None]
            vcons = [tve[0], tve[1]]
            qcons = [None, None]
            aocons = [None, None]
            stok = None
            kc = 0
            g = 0
            for qs in range(nqs):
                for hg in range(8):
                    qpp = g % 2
                    tql = mk.dma("sy", qt[qpp],
                                 q_scr[hg * 128:(hg + 1) * 128,
                                       qs * 512:qs * 512 + 512], deps=[qcons[qpp]],
                                 stream=f"q{qpp}")
                    njc = 4 * qs + 4 if causal else 32
                    nseg = (njc + 7) // 8
                    avtok = [[None, None], [None, None]]
                    sc_last = None
                    av_last = [None, None]
                    for js in range(nseg):
                        jlen = min(8, njc - js * 8)
                        kpp = kc % 2
                        kc += 1
                        tkl = mk.dma("sy", kt[kpp][:, 0:jlen * 128],
                                     k_scr[hg * 128:(hg + 1) * 128,
                                           js * 1024:js * 1024 + jlen * 128],
                                     deps=[kcons[kpp]], stream=f"k{kpp}")
                        for h in range(2):
                            tvl = mk.dma(
                                "gp", vt4[kpp][:, 0:jlen, h:h + 1, 0:64],
                                vsrc[:, js * 8:js * 8 + jlen,
                                     2 * hg + h:2 * hg + h + 1, :],
                                deps=[vcons[kpp]] if h == 0 else [],
                                stream=f"v{kpp}")
                        for jcr in range(jlen):
                            jc = js * 8 + jcr
                            for h in range(2):
                                sb_i = 2 + h * 2 + (jc % 2)
                                sb_ = psum[sb_i]
                                tsmm = mk.op("te", lambda E, o=sb_, kp=kpp, h=h,
                                             j=jcr, qp=qpp: E.matmul(
                                                 o[:],
                                                 kt[kp][h * 64:(h + 1) * 64,
                                                        j * 128:j * 128 + 128],
                                                 qt[qp][h * 64:(h + 1) * 64, :],
                                                 start=True, stop=True,
                                                 skip_group_check=True),
                                             deps=[tkl, tql, mk.banktok[sb_i]])
                                sc_last = tsmm
                                ee = et[h][jc % 2]
                                tex = mk.op("se", lambda E, o=ee, i=sb_:
                                            E.activation(o, i[:], AF.Exp,
                                                         bias=-SHIFT, scale=0.125),
                                            deps=[tsmm, avtok[h][jc % 2]])
                                mk.banktok[sb_i] = tex
                                edep = tex
                                if causal and jc >= 4 * qs:
                                    dd = jc - 4 * qs
                                    edep = mk.op("ve", lambda E, o=ee, d=dd:
                                                 E.tensor_mul(
                                                     o, o,
                                                     maskb[:, d * 512:d * 512 + 512]),
                                                 deps=[tex])
                                acc = psum[h]
                                tav = mk.op("te", lambda E, o=acc, kp=kpp, h=h,
                                            j=jcr, e=ee, jj=jc, nn=njc: E.matmul(
                                                o[0:65, :],
                                                vt[kp][:, j * 130 + h * 65:
                                                       j * 130 + h * 65 + 65],
                                                e, start=(jj == 0), stop=(jj == nn - 1),
                                                skip_group_check=True),
                                            deps=([edep, tvl, mk.banktok[h]]
                                                  if jc == 0 else [edep, tvl]))
                                avtok[h][jc % 2] = tav
                                av_last[h] = tav
                        kcons[kpp] = sc_last
                        vcons[kpp] = av_last[1] or av_last[0]
                    qcons[qpp] = sc_last
                    for h in range(2):
                        acc = psum[h]
                        trc = mk.op("ve", lambda E, i=acc: E.reciprocal(
                            statb[64:65, 0:512], i[64:65, :]),
                            deps=[av_last[h], stok])
                        bb_i = 6 + h
                        tbc = mk.op("te", lambda E, o=psum[bb_i]: E.matmul(
                            o[0:64, :], onesb[64:65, 0:64], statb[64:65, 0:512],
                            start=True, stop=True),
                            deps=[trc, mk.banktok[bb_i]])
                        stok = tbc
                        tcp = mk.op("ve", lambda E, o=lb[h], i=psum[bb_i]:
                                    E.tensor_copy(o[0:64, :], i[0:64, :]),
                                    deps=[tbc])
                        mk.banktok[bb_i] = tcp
                        tmul = mk.op("ve", lambda E, o=ao[h], i=acc, l=lb[h]:
                                     E.tensor_mul(o[0:64, :], i[0:64, :], l[0:64, :]),
                                     deps=[tcp, aocons[h]])
                        mk.banktok[h] = tmul
                        aocons[h] = mk.dma(
                            "gp", ao_scr[hg * 128 + h * 64:hg * 128 + h * 64 + 64,
                                         qs * 512:qs * 512 + 512],
                            ao[h][0:64, :], deps=[tmul], stream=f"e{h}")
                    g += 1
            mk.barrier()

        def ffn_pass(n_st, x_loader, dst_fn, dbg_first=False):
            w13v = [wbuf[:, 0:8192], wbuf[:, 8192:16384]]
            w2v = [wbuf[:, 16384:20480], wbuf[:, 20480:24576]]
            wtoks = [None, None]
            wfree = [None, None]
            for st in range(n_st):
                dbg_st = dbg_first and debug and st == 0
                tx = x_loader(st)
                xntok, _ = ln_tile(tx)
                ltoks = [None] * 4
                for tc in range(4):
                    for k in range(8):
                        lhs = xnbuf[:, k * 512 + tc * 128: k * 512 + tc * 128 + 128]
                        rhs = gwtb[:, k * 16:(k + 1) * 16]
                        ltoks[tc] = mk.op("te", lambda E, l=lhs, r=rhs, k=k, tc=tc:
                                          E.matmul(psum[tc][:, 0:16],
                                                   l, r, start=(k == 0), stop=(k == 7)),
                                          deps=([xntok, mk.banktok[tc]]
                                                if k == 0 else []),
                                          inc=(k == 7))
                combtok = None
                for tc in range(4):
                    sl = slice(tc * 16, tc * 16 + 16)
                    scores = cwork[:, 64 + tc * 16:64 + tc * 16 + 16]
                    b0 = cwork[:, 128 + tc * 16:128 + tc * 16 + 16]
                    gt = cwork[:, 208 + tc * 16:208 + tc * 16 + 16]
                    bk = cwork[:, 272 + tc * 16:272 + tc * 16 + 16]
                    sel = cwork[:, 336 + tc * 16:336 + tc * 16 + 16]
                    wv = cwork[:, 400 + tc * 16:400 + tc * 16 + 16]
                    mcol = cwork[:, 192 + tc:192 + tc + 1]
                    wsc = cwork[:, 196 + tc:196 + tc + 1]
                    wic = cwork[:, 200 + tc:200 + tc + 1]
                    t = mk.op("se", lambda E, o=scores, tc=tc: E.activation(
                        o, psum[tc][:, 0:16], AF.Sigmoid), deps=[ltoks[tc]])
                    t = mk.op("ve", lambda E, o=b0, i=scores: E.tensor_add(o, i, gbb),
                              deps=[t])
                    t = mk.op("ve", lambda E, o=bk, i=b0: E.tensor_copy(o, i),
                              deps=[t])
                    bcur, balt = bk, sel
                    for _ in range(3):
                        t = mk.op("ve", lambda E, i=bcur, m=mcol: E.reduce_max(
                            m, i, AX.X), deps=[t])
                        t = mk.op("ve", lambda E, o=gt, i=bcur, m=mcol:
                                  E.tensor_single_scalar(o, i, m[:, 0:1], ALU.is_ge),
                                  hard_deps=[t])
                        t = mk.op("ve", lambda E, o=balt, g2=gt, i=bcur:
                                  E.scalar_tensor_tensor(
                                      o, g2, -2e9, i, ALU.mult, ALU.add), deps=[t])
                        bcur, balt = balt, bcur
                    t = mk.op("ve", lambda E, i=bcur, m=mcol: E.reduce_max(m, i, AX.X),
                              deps=[t])
                    t = mk.op("ve", lambda E, o=sel, i=b0, m=mcol:
                              E.tensor_single_scalar(o, i, m[:, 0:1], ALU.is_ge),
                              hard_deps=[t])
                    t = mk.op("ve", lambda E, o=wv, i=sel, s=scores: E.tensor_mul(
                        o, i, s), deps=[t])
                    t = mk.op("ve", lambda E, i=wv, w=wsc: E.reduce_sum(w, i, AX.X),
                              deps=[t])
                    t = mk.op("ve", lambda E, o=wic, i=wsc: E.reciprocal(o, i),
                              deps=[t])
                    combtok = mk.op("ve", lambda E, o=cwork[:, sl], i=wv, w=wic:
                                    E.tensor_scalar(o, i, w[:, 0:1], 2.5,
                                                    ALU.mult, ALU.mult),
                                    hard_deps=[t])
                for tc in range(4):
                    mk.banktok[tc] = combtok
                if dbg_st:
                    for tc in range(4):
                        mk.op("ve", lambda E, tc=tc: E.tensor_copy(
                            mrb[:, tc * 16:tc * 16 + 16], psum[tc][:, 0:16]),
                            deps=[ltoks[tc]])
                    tlg = mk.op("ve", lambda E: E.tensor_copy(
                        mrb[:, 64:128], mrb[:, 0:64]))
                    mk.dma("gp", dbg["d_lg"], mrb[:, 0:64], deps=[tlg])
                    mk.dma("gp", dbg["d_xn"], xnbuf[:], deps=[xntok])
                    mk.dma("gp", dbg["d_cw"], cwork[:], deps=[combtok])
                tct = None
                for tc in range(4):
                    ttr = mk.op("te", lambda E, tc=tc: E.transpose(
                        psum[4 + (tc % 2)][0:16, 0:128],
                        cwork[:, tc * 16:tc * 16 + 16], identb[:]),
                        deps=[combtok, mk.banktok[4 + (tc % 2)]])
                    tct = mk.op("ve", lambda E, tc=tc: E.tensor_copy(
                        statb[0:16, tc * 128:tc * 128 + 128],
                        psum[4 + (tc % 2)][0:16, 0:128]), deps=[ttr])
                    mk.banktok[4 + (tc % 2)] = tct
                tflat = None
                cbtok = None
                hwar = []
                w13src = w13t.rearrange("e (k p) m -> e p k m", p=128)
                w2src = w2t.rearrange("e (k p) m -> e p k m", p=128)
                for e in range(NE + 1):
                    if e % 8 == 0 and e < NE:
                        tflat = mk.dma("gp", cflat[0:1, :],
                                       statb[(e // 8) * 8:(e // 8) * 8 + 8, 0:512],
                                       deps=[tct, cbtok], stream="cf")
                        if dbg_st and e == 0:
                            mk.dma("gp", dbg["d_cf"], cflat[0:1, :], deps=[tflat])
                    bi = e % 2
                    mk.dma("sy",
                           w13v[bi].rearrange("p (k m) -> p k m", k=8),
                           w13src[e:e + 1], deps=[wfree[bi]], stream=f"w{bi}")
                    t2l = mk.dma("sy",
                                 w2v[bi].rearrange("p (k m) -> p k m", k=4),
                                 w2src[e:e + 1], stream=f"w{bi}")
                    t13 = t2l
                    if e < NE:
                        cbtok = mk.op("te", lambda E, e=e: E.matmul(
                            psum[6][:], onesb[0:1, 0:128],
                            cflat[0:1, (e % 8) * 512:(e % 8) * 512 + 512],
                            start=True, stop=True),
                            deps=[tflat, mk.banktok[6]])
                    htok = [None] * 4
                    lastmm = None
                    for pi, (mg, mu) in enumerate([(0, 4), (1, 5), (2, 6), (3, 7)]):
                        toks = {}
                        for which, m in (("g", mg), ("u", mu)):
                            pb_i = (pi * 2 + (0 if which == "g" else 1)) % 4
                            pb = psum[pb_i]
                            last = None
                            for k in range(8):
                                lhs = w13v[bi][:, k * 1024 + m * 128:
                                               k * 1024 + m * 128 + 128]
                                rhs = xnbuf[:, k * 512:(k + 1) * 512]
                                last = mk.op("te", lambda E, o=pb, l=lhs, r=rhs, k=k:
                                             E.matmul(o[:], l, r, start=(k == 0),
                                                      stop=(k == 7)),
                                             deps=([xntok, t13, mk.banktok[pb_i]]
                                                   if k == 0 else []),
                                             inc=(k == 7))
                            toks[which] = (last, pb_i)
                        hc = hbuf[:, pi * 512:pi * 512 + 512]
                        tsl = mk.op("se", lambda E, o=hc, i=psum[toks["g"][1]]:
                                    E.activation(o, i[:], AF.Silu),
                                    deps=[toks["g"][0]] + hwar)
                        mk.banktok[toks["g"][1]] = tsl
                        tmu = mk.op("ve", lambda E, o=hc, i=psum[toks["u"][1]]:
                                    E.tensor_mul(o, o, i[:]),
                                    deps=[tsl, toks["u"][0]])
                        mk.banktok[toks["u"][1]] = tmu
                        if e < NE:
                            tmu = mk.op("ve", lambda E, o=hc: E.tensor_mul(
                                o, o, psum[6][:]), deps=[tmu, cbtok])
                        htok[pi] = tmu
                        lastmm = toks["u"][0]
                    if e < NE:
                        mk.banktok[6] = htok[3]
                    if dbg_st and e == 0:
                        tdh = mk.dma("gp", dbg["d_h"], hbuf[:], deps=[htok[3]])
                        hwar_extra = tdh
                    else:
                        hwar_extra = None
                    w2last = None
                    for m in range(8):
                        pb_i = 4 + (m % 2)
                        pb = psum[pb_i]
                        last = None
                        for k in range(4):
                            lhs = w2v[bi][:, k * 1024 + m * 128:
                                          k * 1024 + m * 128 + 128]
                            rhs = hbuf[:, k * 512:(k + 1) * 512]
                            last = mk.op("te", lambda E, o=pb, l=lhs, r=rhs, k=k:
                                         E.matmul(o[:], l, r, start=(k == 0),
                                                  stop=(k == 3)),
                                         deps=([htok[k], t2l, mk.banktok[pb_i]]
                                               if k == 0 else [htok[k]]),
                                         inc=(k == 3))
                        tacc = mk.op("ve", lambda E, m=m, i=pb: E.tensor_add(
                            xbuf[:, m * 512:m * 512 + 512],
                            xbuf[:, m * 512:m * 512 + 512], i[:]), deps=[last])
                        mk.banktok[pb_i] = tacc
                        w2last = last
                    wfree[bi] = w2last
                    hwar = [w2last] + ([hwar_extra] if hwar_extra else [])
                tst = mk.dma("gp", dst_fn(st),
                             xbuf[:].rearrange("p (k t) -> p k t", k=8),
                             deps=[mk.banktok[4], mk.banktok[5]])
                mk.barrier()

        def snap(name, src):
            if debug:
                mk.dma("gp", dbg[name], src)
                mk.barrier()

        qkv_pass(wqkv_a, load_x_from(x_fm), range(8),
                 [(m, q_scr) for m in range(8)] + [(m, k_scr) for m in range(8, 16)],
                 do_v=True, rope_pos=None)

        def scramble_k():
            minp = wbuf[:, 0:4096]
            ev = [wbuf[:, 4096:4608], wbuf[:, 4608:5120]]
            evtok = [None, None]
            prev_tr = None
            for h in range(16):
                hrow = (h // 2) * 128 + (h % 2) * 64
                ldt = None
                for b8 in range(8):
                    src = k_scr[hrow + b8 * 8:hrow + b8 * 8 + 8, :].rearrange(
                        "a (b c) -> b a c", c=64)
                    dst = minp[0:64, b8 * 512:b8 * 512 + 512].rearrange(
                        "p (a c) -> p a c", c=64)
                    ldt = mk.dma("sy", dst, src,
                                 deps=[prev_tr] if b8 == 0 else [], stream="x0")
                ttr = None
                for b8 in range(8):
                    pb_i = b8 % 2
                    pb = psum[pb_i]
                    for a in range(8):
                        ttr = mk.op("te", lambda E, o=pb, b8=b8, a=a: E.transpose(
                            o[0:64, a * 64:a * 64 + 64],
                            minp[0:64, (b8 * 8 + a) * 64:(b8 * 8 + a) * 64 + 64],
                            identb[0:64, 0:64]),
                            deps=([ldt, mk.banktok[pb_i]] if a == 0 else []),
                            inc=(a == 7))
                    tev = mk.op("ve", lambda E, o=ev[pb_i], i=pb: E.tensor_copy(
                        o[0:64, :], i[0:64, :]), deps=[ttr, evtok[pb_i]])
                    mk.banktok[pb_i] = tev
                    evtok[pb_i] = mk.dma(
                        "gp", k_scr[hrow:hrow + 64, b8 * 512:b8 * 512 + 512],
                        ev[pb_i][0:64, :], deps=[tev], stream=f"e{pb_i}")
                prev_tr = ttr
            mk.barrier()

        scramble_k()
        snap("d_q", q_scr)
        snap("d_k", k_scr)
        snap("d_vt", vt_scr)
        attn_pass(8, causal=True)
        snap("d_ao", ao_scr)

        def ffn1_loader(st):
            t1 = load_x_from(x_fm)(st)
            t2 = mk.dma("sy", xnbuf[:].rearrange("p (k t) -> p k t", k=8),
                        ao_scr.rearrange("(k p) t -> p k t", p=128)
                        [:, :, st * 512:(st + 1) * 512], stream="x0")
            tok = None
            for k in range(NCH):
                tok = mk.op("ve", lambda E, k=k: E.tensor_add(
                    xbuf[:, k * 512:k * 512 + 512],
                    xbuf[:, k * 512:k * 512 + 512],
                    xnbuf[:, k * 512:k * 512 + 512]), deps=[t1, t2])
            return tok

        ffn_pass(8, ffn1_loader,
                 lambda st: x2_scr.rearrange("(k p) t -> p k t", p=128)
                 [:, :, st * 512:(st + 1) * 512], dbg_first=True)
        snap("d_x2", x2_scr)

        qkv_pass(wqkv_b, load_x_from(x2_scr), range(8),
                 [(m, k_scr) for m in range(8, 16)], do_v=True,
                 rope_pos=(posf_d, 0))
        snap("d_k2", k_scr)
        snap("d_vt2", vt_scr)
        qkv_pass(wqkv_b, load_x2my, range(4),
                 [(m, q_scr) for m in range(8)], do_v=False,
                 rope_pos=(posq_d, 0))
        snap("d_q2", q_scr)
        attn_pass(4, causal=False)
        snap("d_ao2", ao_scr)

        wt = None
        for k in range(8):
            wt = mk.dma("sy", wbuf[:, k * 1024:(k + 1) * 1024],
                        wo_d[k * 128:(k + 1) * 128, :], stream="w0")
        for tt in range(4):
            tb = load_x2my(tt)
            tao = mk.dma("sy", xnbuf[:].rearrange("p (k t) -> p k t", k=8),
                         ao_scr.rearrange("(k p) t -> p k t", p=128)
                         [:, :, tt * 512:(tt + 1) * 512], deps=[tb], stream="x0")
            for m in range(8):
                pb_i = m % 2
                pb = psum[pb_i]
                last = None
                for k in range(8):
                    lhs = wbuf[:, k * 1024 + m * 128: k * 1024 + m * 128 + 128]
                    rhs = xnbuf[:, k * 512:(k + 1) * 512]
                    last = mk.op("te", lambda E, o=pb, l=lhs, r=rhs, k=k:
                                 E.matmul(o[:], l, r, start=(k == 0), stop=(k == 7)),
                                 deps=([tao, wt, mk.banktok[pb_i]] if k == 0 else []),
                                 inc=(k == 7))
                tacc = mk.op("ve", lambda E, m=m, i=pb: E.tensor_add(
                    xbuf[:, m * 512:m * 512 + 512],
                    xbuf[:, m * 512:m * 512 + 512], i[:]), deps=[last])
                mk.banktok[pb_i] = tacc
            mk.dma("gp", x3_scr.rearrange("(k p) t -> p k t", p=128)
                   [:, :, tt * 512:(tt + 1) * 512],
                   xbuf[:].rearrange("p (k t) -> p k t", k=8),
                   deps=[mk.banktok[0], mk.banktok[1]])
            mk.barrier()
        snap("d_x3", x3_scr)

        ffn_pass(4, load_x_from(x3_scr),
                 lambda st: y_out.rearrange("(k p) t -> p k t", p=128)
                 [:, :, st * 512:(st + 1) * 512])

        mk.barrier()

        def make_runner(name):
            def body(E):
                for (e, fn, waits, incspec) in mk.ops:
                    if e != name:
                        continue
                    for (se_, v) in waits:
                        E.wait_ge(sems[se_], v)
                    if fn is None:
                        continue
                    inst = fn(E)
                    if incspec is not None:
                        sname, n = incspec
                        inst.then_inc(sems[sname], n)
            return body

        block.tensor(make_runner("te"))
        block.vector(make_runner("ve"))
        block.scalar(make_runner("se"))
        block.sync(make_runner("sy"))
        block.gpsimd(make_runner("gp"))

    return nc


_NC_CACHE = None


def kernel(**inputs):
    global _NC_CACHE
    x = np.asarray(inputs["L0_attn_a_x"], dtype=np.float32)
    wqkv_a = np.ascontiguousarray(inputs["L0_attn_a_qkv_weights"], dtype=np.float32)
    wqkv_b = np.ascontiguousarray(inputs["L0_attn_b_w_qkv"], dtype=np.float32)
    wo = np.ascontiguousarray(inputs["L0_attn_b_w_o"], dtype=np.float32)
    gate_w = np.asarray(inputs["L0_ffn_gate_w"], dtype=np.float32)
    gate_b = np.asarray(inputs["L0_ffn_gate_b"], dtype=np.float32)
    w13 = np.asarray(inputs["L0_ffn_w13"], dtype=np.float32)
    w2 = np.asarray(inputs["L0_ffn_w2"], dtype=np.float32)
    w13s = np.asarray(inputs["L0_ffn_w13_s"], dtype=np.float32)
    w2s = np.asarray(inputs["L0_ffn_w2_s"], dtype=np.float32)

    w13t = np.ascontiguousarray(
        np.concatenate([w13.transpose(0, 2, 1), w13s.T[None]], axis=0))
    w2t = np.ascontiguousarray(
        np.concatenate([w2.transpose(0, 2, 1), w2s.T[None]], axis=0))
    gwt = np.ascontiguousarray(gate_w.T)
    gbv = gate_b.reshape(1, NE)

    jj = np.arange(128)[:, None]
    qq = np.arange(512)[None, :]
    masks = np.concatenate(
        [(jj + d * 128 <= qq).astype(np.float32) for d in range(4)], axis=1)
    ident = np.eye(128, dtype=np.float32)
    prot = np.zeros((128, 128), dtype=np.float32)
    for base in (0, 64):
        for m in range(32):
            prot[base + m + 32, base + m] = -1.0
            prot[base + m, base + m + 32] = 1.0
    irep = (np.arange(128) % 32).astype(np.float32).reshape(1, 128)
    pos_full = np.arange(S, dtype=np.float32).reshape(1, S)

    if _NC_CACHE is None:
        _NC_CACHE = build_program()
    nc = _NC_CACHE

    in_maps = []
    for c in range(8):
        b, hc = c // 2, c % 2
        in_maps.append({
            "x_fm": np.ascontiguousarray(x[b].T),
            "wqkv_a": wqkv_a, "wqkv_b": wqkv_b, "wo": wo,
            "gwt": gwt, "gb": gbv, "w13t": w13t, "w2t": w2t,
            "masks": masks, "ident": ident, "prot": prot, "irep": irep,
            "pos_full": pos_full,
            "pos_q": (hc * HALF + np.arange(HALF, dtype=np.float32)).reshape(1, HALF),
            "hcv": np.full((128, 1), float(hc), dtype=np.float32),
        })

    res = run_bass_kernel_spmd(nc, in_maps, core_ids=list(range(8)))
    ys = [np.array(res.results[c]["y"], copy=True) for c in range(8)]
    out = np.empty((4, S, D), dtype=np.float32)
    for c in range(8):
        b, hc = c // 2, c % 2
        out[b, hc * HALF:(hc + 1) * HALF, :] = ys[c].T
    return out


# version 08 (final)
# speedup vs baseline: 1.0587x; 1.0587x over previous
import math
import numpy as np
from contextlib import ExitStack

import concourse.bass as bass
import concourse.mybir as mybir
from concourse.bass_utils import run_bass_kernel_spmd

F32 = mybir.dt.float32
BF16 = mybir.dt.bfloat16
I32 = mybir.dt.int32
AF = mybir.ActivationFunctionType
ALU = mybir.AluOpType
AX = mybir.AxisListType

D = 1024
S = 4096
HALF = 2048
NCH = 8
NE = 16
SHIFT = 16.0
TWO_PI = 2.0 * math.pi
PI = math.pi

ENGS = ("te", "ve", "se", "sy", "gp")
DMA_STREAMS = ("w0", "w1", "x0", "k0", "k1", "v0", "v1", "q0", "q1",
               "e0", "e1", "st", "cf", "pos", "ini")
ALL_SEMS = ("te", "ve", "se") + DMA_STREAMS
CONST_VALS = [0.0, 1.0, -1.0, 0.125, -SHIFT, 1e-5, -0.5]


def _hl(x):
    hi = np.float32(x)
    lo = np.float32(np.float64(x) - np.float64(hi))
    return float(hi), float(lo)


C_LNTH = _hl(-math.log(10000.0) / 32.0)
C_2PI = _hl(TWO_PI)


class MK:
    def __init__(self):
        self.nc = bass.Bass("TRN2", target_bir_lowering=False)
        self.ops = []
        self.tick = {e: 0 for e in ALL_SEMS}
        self.water = {}
        self.banktok = [None] * 8

    def op(self, eng, fn, deps=(), inc=True, hard_deps=()):
        waits = []
        agg = {}
        for d in deps:
            if d is None:
                continue
            e2, v = d
            if e2 == eng and e2 == "te":
                continue
            agg[e2] = max(agg.get(e2, 0), v)
        for d in hard_deps:
            if d is None:
                continue
            e2, v = d
            agg[e2] = max(agg.get(e2, 0), v)
        for e2, v in agg.items():
            key = (eng, e2)
            if v > self.water.get(key, -1):
                waits.append((e2, v))
                self.water[key] = v
        tok = None
        n = 0
        if inc:
            n = 1
            self.tick[eng] += n
            tok = (eng, self.tick[eng])
        self.ops.append((eng, fn, waits, (eng, n) if n else None))
        return tok

    def dma(self, q, out_ap, in_ap, deps=(), stream="st"):
        waits = []
        agg = {}
        for d in deps:
            if d is None:
                continue
            e2, v = d
            agg[e2] = max(agg.get(e2, 0), v)
        for e2, v in agg.items():
            key = (q, e2)
            if v > self.water.get(key, -1):
                waits.append((e2, v))
                self.water[key] = v
        self.tick[stream] += 16
        tok = (stream, self.tick[stream])
        self.ops.append((q, lambda E, o=out_ap, i=in_ap: E.dma_start(o, i),
                         waits, (stream, 16)))
        return tok

    def barrier(self):
        snap = dict(self.tick)
        for e in ENGS:
            deps = [(e2, v) for e2, v in snap.items() if v > 0 and e2 != e]
            self.op(e, None, deps, inc=False)


def build_program(debug=False):
    mk = MK()
    nc = mk.nc

    def din(name, shape, dt=F32):
        return nc.dram_tensor(name, shape, dt, kind="ExternalInput").ap()

    x_fm = din("x_fm", [D, S])
    wqkv_a = din("wqkv_a", [D, 3 * D])
    wqkv_b = din("wqkv_b", [D, 3 * D])
    wo_d = din("wo", [D, D])
    gwt = din("gwt", [D, NE])
    gb = din("gb", [1, NE])
    w13t = din("w13t", [NE + 1, D, D])
    w2t = din("w2t", [NE + 1, 512, D])
    masks_d = din("masks", [128, 2048])
    ident_d = din("ident", [128, 128])
    prot_d = din("prot", [128, 128])
    irep_d = din("irep", [1, 128])
    posf_d = din("pos_full", [1, S])
    posq_d = din("pos_q", [1, HALF])
    hcv_d = din("hcv", [128, 1])

    y_out = nc.dram_tensor("y", [D, HALF], F32, kind="ExternalOutput").ap()

    def scr(name, shape):
        return nc.dram_tensor(name, shape, F32, kind="Internal").ap()

    q_scr = scr("q_scr", [D, S])
    k_scr = scr("k_scr", [D, S])
    vt_scr = scr("vt_scr", [S, D])
    ao_scr = scr("ao_scr", [D, S])
    x2_scr = scr("x2_scr", [D, S])
    x3_scr = scr("x3_scr", [D, HALF])

    dbg = {}
    if debug:
        for nm, shape in [("d_q", [D, S]), ("d_k", [D, S]), ("d_vt", [S, D]),
                          ("d_ao", [D, S]), ("d_x2", [D, S]), ("d_k2", [D, S]),
                          ("d_vt2", [S, D]), ("d_q2", [D, S]), ("d_ao2", [D, S]),
                          ("d_x3", [D, HALF]),
                          ("d_xn", [128, 4096]), ("d_cw", [128, 512]),
                          ("d_lg", [128, 64]),
                          ("d_cf", [1, 4096]), ("d_h", [128, 2048])]:
            dbg[nm] = nc.dram_tensor(nm, shape, F32, kind="ExternalOutput").ap()

    with ExitStack() as ctx:
        def sbt(name, shape, dt=F32):
            return ctx.enter_context(nc.sbuf_tensor(name, shape, dt))

        wbuf = sbt("wbuf", [128, 24576])
        xbuf = sbt("xbuf", [128, 4096])
        xnbuf = sbt("xnbuf", [128, 4096])
        hbuf = sbt("hbuf", [128, 2048])
        cbuf = sbt("cbuf", [128, 512])
        mrb = sbt("mrb", [128, 512])
        maskb = sbt("maskb", [128, 2048])
        identb = sbt("identb", [128, 128])
        onesb = sbt("onesb", [128, 128])
        protb = sbt("protb", [128, 128])
        gwtb = sbt("gwtb", [128, 144])
        statb = sbt("statb", [128, 1024])
        mbc = sbt("mbc", [128, 512])
        rbc = sbt("rbc", [128, 512])
        cwork = sbt("cwork", [128, 512])
        ropei = sbt("ropei", [128, 512], I32)
        wrh = sbt("wrh", [128, 2048], BF16)
        wrl = sbt("wrl", [128, 2048], BF16)
        xrh = sbt("xrh", [128, 1024], BF16)
        xrl = sbt("xrl", [128, 1024], BF16)
        hhb = sbt("hhb", [128, 2048], BF16)
        hlb = sbt("hlb", [128, 2048], BF16)
        constb = sbt("constb", [128, 448])
        hcvb = sbt("hcvb", [128, 1])

        for i, v in enumerate(CONST_VALS):
            nc.const_aps.register(F32, float(v), constb[:, i:i + 1])

        psum = [ctx.enter_context(nc.psum_tensor(f"pb{i}", [128, 512], F32))
                for i in range(8)]
        sems = {e: ctx.enter_context(nc.semaphore(f"sem_{e}")) for e in ALL_SEMS}
        block = ctx.enter_context(nc.Block())

        irepb = constb[0:1, 16:144]
        iftmp = constb[0:1, 144:272]
        ifrow = constb[0:1, 272:400]
        gbsb = constb[0:1, 400:416]
        posb = statb[0:1, 512:1024]
        gbb = gwtb[:, 128:144]

        def ln_tile(xdep, extra_dep=None):
            tsum = None
            for k in range(NCH):
                xc = xbuf[:, k * 512:(k + 1) * 512]
                tsum = mk.op("te", lambda E, i=xc, k=k: E.matmul(
                    psum[6][0:1, :], onesb[:, 0:1], i,
                    start=(k == 0), stop=(k == NCH - 1)),
                    deps=([xdep, extra_dep, mk.banktok[6]] if k == 0 else []),
                    inc=(k == NCH - 1))
            sqtoks = []
            for k in range(NCH):
                xc = xbuf[:, k * 512:(k + 1) * 512]
                sq = xnbuf[:, k * 512:(k + 1) * 512]
                sqtoks.append(mk.op("se", lambda E, o=sq, i=xc: E.activation(
                    o, i, AF.Square), deps=[xdep, extra_dep]))
            tsq = None
            for k in range(NCH):
                sq = xnbuf[:, k * 512:(k + 1) * 512]
                tsq = mk.op("te", lambda E, i=sq, k=k: E.matmul(
                    psum[7][0:1, :], onesb[:, 0:1], i,
                    start=(k == 0), stop=(k == NCH - 1)),
                    deps=([sqtoks[k], mk.banktok[7]] if k == 0 else [sqtoks[k]]),
                    inc=(k == NCH - 1))
            mean = statb[0:1, 0:512]
            m2 = statb[0:1, 512:1024]
            t1 = mk.op("ve", lambda E: E.tensor_scalar_mul(
                mean, psum[6][0:1, :], 1.0 / D), deps=[tsum])
            tbm = mk.op("te", lambda E: E.matmul(
                psum[4][:], onesb[0:1, 0:128], mean, start=True, stop=True),
                deps=[t1, mk.banktok[4]])
            tmb = mk.op("ve", lambda E: E.tensor_copy(mbc[:], psum[4][:]), deps=[tbm])
            mk.banktok[4] = tmb
            t2 = mk.op("ve", lambda E: E.tensor_mul(m2, mean, mean), deps=[t1])
            tvar = mk.op("ve", lambda E: E.scalar_tensor_tensor(
                mean, psum[7][0:1, :], 1.0 / D, m2, ALU.mult, ALU.subtract),
                deps=[tsq, t2, tbm])
            mk.banktok[6] = t1
            mk.banktok[7] = tvar
            tln = mk.op("se", lambda E: E.activation(m2, mean, AF.Ln, bias=1e-5),
                        deps=[tvar])
            trs = mk.op("se", lambda E: E.activation(mean, m2, AF.Exp, scale=-0.5),
                        deps=[tln])
            tbr = mk.op("te", lambda E: E.matmul(
                psum[5][:], onesb[0:1, 0:128], mean, start=True, stop=True),
                deps=[trs, mk.banktok[5]])
            trb = mk.op("ve", lambda E: E.tensor_copy(rbc[:], psum[5][:]), deps=[tbr])
            mk.banktok[5] = trb
            tmr = mk.op("ve", lambda E: E.tensor_mul(mrb[:], mbc[:], rbc[:]),
                        deps=[tmb, trb])
            xt = None
            for k in range(NCH):
                xc = xbuf[:, k * 512:(k + 1) * 512]
                xn = xnbuf[:, k * 512:(k + 1) * 512]
                tm = mk.op("ve", lambda E, o=xn, i=xc: E.tensor_mul(o, i, rbc[:]),
                           deps=[trb, tvar])
                xt = mk.op("ve", lambda E, o=xn: E.tensor_sub(o, o, mrb[:]),
                           deps=[tmr, tm])
            return xt, tbr

        def rope_tables(pos_src, pos_off, dep):
            tp = mk.dma("sy", posb, pos_src[0:1, pos_off:pos_off + 512], deps=[dep],
                        stream="pos")
            targ = mk.op("te", lambda E: E.matmul(
                psum[6][:], ifrow, posb, start=True, stop=True),
                deps=[tp, mk.banktok[6]])
            w1 = hbuf[:, 1024:1536]
            w2 = hbuf[:, 1536:2048]
            w3 = mrb[:, 0:512]
            toks = []
            last = None
            for which, tbl, extra in (("sin", rbc, 0.5), ("cos", mbc, 0.75)):
                t0 = mk.op("ve", lambda E, x=extra: E.tensor_scalar(
                    w1, psum[6][:], 1.0 / TWO_PI, x, ALU.mult, ALU.add),
                    deps=[targ, last])
                tc1 = mk.op("ve", lambda E: E.tensor_copy(ropei[:], w1), deps=[t0])
                tc2 = mk.op("ve", lambda E: E.tensor_copy(w1, ropei[:]), deps=[tc1])
                t = mk.op("ve", lambda E: E.scalar_tensor_tensor(
                    w2, w1, C_2PI[0], psum[6][:], ALU.mult, ALU.subtract), deps=[tc2])
                t = mk.op("ve", lambda E: E.scalar_tensor_tensor(
                    w3, w1, C_2PI[1], w2, ALU.mult, ALU.add), deps=[t])
                cur, alt = w3, w2
                if which == "cos":
                    t = mk.op("ve", lambda E, o=alt, i=cur: E.tensor_scalar_add(
                        o, i, -PI / 2.0), deps=[t])
                    cur, alt = alt, cur
                t = mk.op("ve", lambda E, i=cur: E.tensor_single_scalar(
                    w1, i, PI, ALU.is_gt), deps=[t])
                t = mk.op("ve", lambda E, o=alt, i=cur: E.scalar_tensor_tensor(
                    o, w1, -TWO_PI, i, ALU.mult, ALU.add), deps=[t])
                cur, alt = alt, cur
                t = mk.op("ve", lambda E, i=cur: E.tensor_single_scalar(
                    w1, i, -PI, ALU.is_lt), deps=[t])
                t = mk.op("ve", lambda E, o=alt, i=cur: E.scalar_tensor_tensor(
                    o, w1, TWO_PI, i, ALU.mult, ALU.add), deps=[t])
                cur, alt = alt, cur
                ts = mk.op("se", lambda E, o=tbl, i=cur: E.activation(
                    o[:], i, AF.Sin, scale=-1.0), deps=[t])
                last = t
                toks.append(ts)
            mk.banktok[6] = toks[-1]
            return toks[0], toks[1]

        for i, v in enumerate(CONST_VALS):
            mk.op("ve", lambda E, i=i, v=v: E.memset(constb[:, i:i + 1], float(v)),
                  inc=False)
        t_ones = mk.op("ve", lambda E: E.memset(onesb[:], 1.0))
        mk.dma("sy", identb[:], ident_d, stream="ini")
        mk.dma("sy", protb[:], prot_d, stream="ini")
        mk.dma("sy", maskb[:], masks_d, stream="ini")
        mk.dma("sy", gwtb[:, 0:128].rearrange("p (k e) -> p k e", k=8),
               gwt.rearrange("(k p) e -> p k e", p=128), stream="ini")
        mk.dma("sy", irepb, irep_d, stream="ini")
        mk.dma("sy", gbsb, gb, stream="ini")
        t_ini = mk.dma("sy", hcvb[:], hcv_d, stream="ini")
        t_ir = t_gb = t_ini
        t_bb = mk.op("te", lambda E: E.matmul(
            psum[6][:, 0:16], onesb[0:1, 0:128], gbsb, start=True, stop=True),
            deps=[t_gb, t_ones])
        tgbb = mk.op("ve", lambda E: E.tensor_copy(gbb, psum[6][:, 0:16]), deps=[t_bb])
        mk.banktok[6] = tgbb
        ti1 = mk.op("ve", lambda E: E.tensor_scalar_mul(iftmp, irepb, C_LNTH[0]),
                    deps=[t_ir])
        ti2 = mk.op("ve", lambda E: E.scalar_tensor_tensor(
            ifrow, irepb, C_LNTH[1], iftmp, ALU.mult, ALU.add), deps=[ti1])
        mk.op("se", lambda E: E.activation(ifrow, ifrow, AF.Exp), deps=[ti2])
        mk.barrier()

        def load_x_from(src):
            def f(tt):
                return mk.dma("sy", xbuf[:].rearrange("p (k t) -> p k t", k=8),
                              src.rearrange("(k p) t -> p k t", p=128)
                              [:, :, tt * 512:(tt + 1) * 512], stream="x0")
            return f

        def load_x2my(tt):
            tA = mk.dma("sy", xbuf[:].rearrange("p (k t) -> p k t", k=8),
                        x2_scr.rearrange("(k p) t -> p k t", p=128)
                        [:, :, tt * 512:(tt + 1) * 512], stream="x0")
            tB = mk.dma("sy", xnbuf[:].rearrange("p (k t) -> p k t", k=8),
                        x2_scr.rearrange("(k p) t -> p k t", p=128)
                        [:, :, HALF + tt * 512:HALF + tt * 512 + 512], stream="x0")
            tok = None
            for k in range(NCH):
                xa = xbuf[:, k * 512:(k + 1) * 512]
                xb = xnbuf[:, k * 512:(k + 1) * 512]
                t1 = mk.op("ve", lambda E, a=xa, b=xb: E.tensor_sub(b, b, a),
                           deps=[tA, tB])
                t2 = mk.op("ve", lambda E, b=xb: E.tensor_scalar_mul(
                    mrb[:], b, hcvb[:, 0:1]), deps=[t1])
                tok = mk.op("ve", lambda E, a=xa: E.tensor_add(a, a, mrb[:]),
                            deps=[t2])
            return tok

        def qkv_pass(w_dram, x_src_fn, ttiles, mspec, do_v, rope_pos):
            wt = None
            for k in range(8):
                wt = mk.dma("sy", wbuf[:, k * 3072:(k + 1) * 3072],
                            w_dram[k * 128:(k + 1) * 128, :], stream="w0")
            ev = [cbuf[:, 0:512], hbuf[:, 0:512]]
            rot = hbuf[:, 512:1024]
            rottok = None
            pp = 0
            for tt in ttiles:
                xdep = x_src_fn(tt)
                xntok, statfree = ln_tile(xdep, extra_dep=wt)
                tsin = tcos = None
                if rope_pos is not None:
                    tsin, tcos = rope_tables(rope_pos[0], rope_pos[1] + tt * 512,
                                             statfree)
                evtok = [None, None]
                evi = 0
                for (m, dst) in mspec:
                    pb_i = pp % 4
                    pb = psum[pb_i]
                    last = None
                    for k in range(8):
                        lhs = wbuf[:, k * 3072 + m * 128: k * 3072 + m * 128 + 128]
                        rhs = xnbuf[:, k * 512:(k + 1) * 512]
                        last = mk.op("te", lambda E, o=pb, l=lhs, r=rhs, k=k:
                                     E.matmul(o[:], l, r, start=(k == 0), stop=(k == 7)),
                                     deps=([xntok, evtok[evi], mk.banktok[pb_i]]
                                           if k == 0 else []),
                                     inc=(k == 7))
                    e = ev[evi]
                    tev = mk.op("ve", lambda E, o=e, i=pb: E.tensor_copy(o, i[:]),
                                deps=[last, evtok[evi]])
                    mk.banktok[pb_i] = tev
                    if rope_pos is not None:
                        rb_i = 4 + (pp % 2)
                        prr = psum[rb_i]
                        trm = mk.op("te", lambda E, o=prr, i=e: E.matmul(
                            o[:], protb[:], i, start=True, stop=True),
                            deps=[tev, mk.banktok[rb_i]])
                        tt1 = mk.op("ve", lambda E, o=e: E.tensor_mul(o, o, mbc[:]),
                                    deps=[tcos, trm])
                        tt2 = mk.op("ve", lambda E, i=prr: E.tensor_mul(
                            rot, i[:], rbc[:]), deps=[trm, tsin, rottok])
                        mk.banktok[rb_i] = tt2
                        tev = mk.op("ve", lambda E, o=e: E.tensor_add(o, o, rot),
                                    deps=[tt1, tt2])
                        rottok = tev
                    hp = m % 8
                    tst = mk.dma("gp", dst[hp * 128:(hp + 1) * 128,
                                           tt * 512:tt * 512 + 512], e, deps=[tev],
                                 stream=f"e{evi}")
                    evtok[evi] = tst
                    evi ^= 1
                    pp += 1
                if do_v:
                    for tc in range(4):
                        for n in range(2):
                            pb_i = 4 + ((tc * 2 + n) % 2)
                            pb = psum[pb_i]
                            last = None
                            for k in range(8):
                                lhs = xnbuf[:, k * 512 + tc * 128:
                                            k * 512 + tc * 128 + 128]
                                rhs = wbuf[:, k * 3072 + 2048 + n * 512:
                                           k * 3072 + 2048 + n * 512 + 512]
                                last = mk.op("te", lambda E, o=pb, l=lhs, r=rhs, k=k:
                                             E.matmul(o[:], l, r, start=(k == 0),
                                                      stop=(k == 7)),
                                             deps=([xntok, evtok[evi],
                                                    mk.banktok[pb_i]]
                                                   if k == 0 else []),
                                             inc=(k == 7))
                            e = ev[evi]
                            tev = mk.op("ve", lambda E, o=e, i=pb: E.tensor_copy(
                                o, i[:]), deps=[last, evtok[evi]])
                            mk.banktok[pb_i] = tev
                            r0 = tt * 512 + tc * 128
                            tst = mk.dma("gp", vt_scr[r0:r0 + 128,
                                                      n * 512:n * 512 + 512], e,
                                         deps=[tev], stream=f"e{evi}")
                            evtok[evi] = tst
                            evi ^= 1
                mk.barrier()

        def attn_pass(nqs, causal):
            kt = [wbuf[:, 0:1024], wbuf[:, 1024:2048]]
            vt_base = [2048, 3088]
            vt = [wbuf[:, b:b + 1040] for b in vt_base]
            vt4 = [wbuf[:, b:b + 1040].rearrange("p (j h c) -> p j h c", j=8, h=2)
                   for b in vt_base]
            qt = [wbuf[:, 4128:4640], wbuf[:, 4640:5152]]
            et = [[wbuf[:, 5152 + (h * 2 + i) * 512: 5152 + (h * 2 + i) * 512 + 512]
                   for i in range(2)] for h in range(2)]
            ao = [wbuf[:, 7200:7712], wbuf[:, 7712:8224]]
            lb = [hbuf[:, 0:512], hbuf[:, 512:1024]]
            vsrc = vt_scr.rearrange("(J p) (H c) -> p J H c", p=128, c=64)
            tve = [None, None]
            for i in range(2):
                tve[i] = mk.op("ve", lambda E, i=i: E.memset(
                    vt4[i][:, :, :, 64:65], 1.0))
            kcons = [None, None]
            vcons = [tve[0], tve[1]]
            qcons = [None, None]
            aocons = [None, None]
            stok = None
            kc = 0
            g = 0
            for qs in range(nqs):
                for hg in range(8):
                    qpp = g % 2
                    tql = mk.dma("sy", qt[qpp],
                                 q_scr[hg * 128:(hg + 1) * 128,
                                       qs * 512:qs * 512 + 512], deps=[qcons[qpp]],
                                 stream=f"q{qpp}")
                    njc = 4 * qs + 4 if causal else 32
                    nseg = (njc + 7) // 8
                    avtok = [[None, None], [None, None]]
                    sc_last = None
                    av_last = [None, None]
                    for js in range(nseg):
                        jlen = min(8, njc - js * 8)
                        kpp = kc % 2
                        kc += 1
                        tkl = mk.dma("sy", kt[kpp][:, 0:jlen * 128],
                                     k_scr[hg * 128:(hg + 1) * 128,
                                           js * 1024:js * 1024 + jlen * 128],
                                     deps=[kcons[kpp]], stream=f"k{kpp}")
                        for h in range(2):
                            tvl = mk.dma(
                                "gp", vt4[kpp][:, 0:jlen, h:h + 1, 0:64],
                                vsrc[:, js * 8:js * 8 + jlen,
                                     2 * hg + h:2 * hg + h + 1, :],
                                deps=[vcons[kpp]] if h == 0 else [],
                                stream=f"v{kpp}")
                        for jcr in range(jlen):
                            jc = js * 8 + jcr
                            for h in range(2):
                                sb_i = 2 + h * 2 + (jc % 2)
                                sb_ = psum[sb_i]
                                tsmm = mk.op("te", lambda E, o=sb_, kp=kpp, h=h,
                                             j=jcr, qp=qpp: E.matmul(
                                                 o[:],
                                                 kt[kp][h * 64:(h + 1) * 64,
                                                        j * 128:j * 128 + 128],
                                                 qt[qp][h * 64:(h + 1) * 64, :],
                                                 start=True, stop=True,
                                                 skip_group_check=True),
                                             deps=[tkl, tql, mk.banktok[sb_i]])
                                sc_last = tsmm
                                ee = et[h][jc % 2]
                                tex = mk.op("se", lambda E, o=ee, i=sb_:
                                            E.activation(o, i[:], AF.Exp,
                                                         bias=-SHIFT, scale=0.125),
                                            deps=[tsmm, avtok[h][jc % 2]])
                                mk.banktok[sb_i] = tex
                                edep = tex
                                if causal and jc >= 4 * qs:
                                    dd = jc - 4 * qs
                                    edep = mk.op("ve", lambda E, o=ee, d=dd:
                                                 E.tensor_mul(
                                                     o, o,
                                                     maskb[:, d * 512:d * 512 + 512]),
                                                 deps=[tex])
                                acc = psum[h]
                                tav = mk.op("te", lambda E, o=acc, kp=kpp, h=h,
                                            j=jcr, e=ee, jj=jc, nn=njc: E.matmul(
                                                o[0:65, :],
                                                vt[kp][:, j * 130 + h * 65:
                                                       j * 130 + h * 65 + 65],
                                                e, start=(jj == 0), stop=(jj == nn - 1),
                                                skip_group_check=True),
                                            deps=([edep, tvl, mk.banktok[h]]
                                                  if jc == 0 else [edep, tvl]))
                                avtok[h][jc % 2] = tav
                                av_last[h] = tav
                        kcons[kpp] = sc_last
                        vcons[kpp] = av_last[1] or av_last[0]
                    qcons[qpp] = sc_last
                    for h in range(2):
                        acc = psum[h]
                        trc = mk.op("ve", lambda E, i=acc: E.reciprocal(
                            statb[64:65, 0:512], i[64:65, :]),
                            deps=[av_last[h], stok])
                        bb_i = 6 + h
                        tbc = mk.op("te", lambda E, o=psum[bb_i]: E.matmul(
                            o[0:64, :], onesb[64:65, 0:64], statb[64:65, 0:512],
                            start=True, stop=True),
                            deps=[trc, mk.banktok[bb_i]])
                        stok = tbc
                        tcp = mk.op("ve", lambda E, o=lb[h], i=psum[bb_i]:
                                    E.tensor_copy(o[0:64, :], i[0:64, :]),
                                    deps=[tbc])
                        mk.banktok[bb_i] = tcp
                        tmul = mk.op("ve", lambda E, o=ao[h], i=acc, l=lb[h]:
                                     E.tensor_mul(o[0:64, :], i[0:64, :], l[0:64, :]),
                                     deps=[tcp, aocons[h]])
                        mk.banktok[h] = tmul
                        aocons[h] = mk.dma(
                            "gp", ao_scr[hg * 128 + h * 64:hg * 128 + h * 64 + 64,
                                         qs * 512:qs * 512 + 512],
                            ao[h][0:64, :], deps=[tmul], stream=f"e{h}")
                    g += 1
            mk.barrier()

        def ffn_pass(n_st, x_loader, dst_fn, dbg_first=False):
            w13v = [wbuf[:, 0:8192], wbuf[:, 8192:16384]]
            w2v = [wbuf[:, 16384:20480], wbuf[:, 20480:24576]]
            wfree = [None, None]
            slotmm = [None, None]
            hfree = [None] * 4
            hbuf_free = [None] * 4
            mrb_rd = None
            cb_dma_ok = None
            for st in range(n_st):
                dbg_st = dbg_first and debug and st == 0
                tx = x_loader(st)
                xntok, _ = ln_tile(tx)
                ltoks = [None] * 4
                for tc in range(4):
                    for k in range(8):
                        lhs = xnbuf[:, k * 512 + tc * 128: k * 512 + tc * 128 + 128]
                        rhs = gwtb[:, k * 16:(k + 1) * 16]
                        ltoks[tc] = mk.op("te", lambda E, l=lhs, r=rhs, k=k, tc=tc:
                                          E.matmul(psum[tc][:, 0:16],
                                                   l, r, start=(k == 0), stop=(k == 7)),
                                          deps=([xntok, mk.banktok[tc]]
                                                if k == 0 else []),
                                          inc=(k == 7))
                combtok = None
                for tc in range(4):
                    sl = slice(tc * 16, tc * 16 + 16)
                    scores = cwork[:, 64 + tc * 16:64 + tc * 16 + 16]
                    b0 = cwork[:, 128 + tc * 16:128 + tc * 16 + 16]
                    gt = cwork[:, 208 + tc * 16:208 + tc * 16 + 16]
                    bk = cwork[:, 272 + tc * 16:272 + tc * 16 + 16]
                    sel = cwork[:, 336 + tc * 16:336 + tc * 16 + 16]
                    wv = cwork[:, 400 + tc * 16:400 + tc * 16 + 16]
                    mcol = cwork[:, 192 + tc:192 + tc + 1]
                    wsc = cwork[:, 196 + tc:196 + tc + 1]
                    wic = cwork[:, 200 + tc:200 + tc + 1]
                    t = mk.op("se", lambda E, o=scores, tc=tc: E.activation(
                        o, psum[tc][:, 0:16], AF.Sigmoid), deps=[ltoks[tc]])
                    t = mk.op("ve", lambda E, o=b0, i=scores: E.tensor_add(o, i, gbb),
                              deps=[t])
                    t = mk.op("ve", lambda E, o=bk, i=b0: E.tensor_copy(o, i),
                              deps=[t])
                    bcur, balt = bk, sel
                    for _ in range(3):
                        t = mk.op("ve", lambda E, i=bcur, m=mcol: E.reduce_max(
                            m, i, AX.X), deps=[t])
                        t = mk.op("ve", lambda E, o=gt, i=bcur, m=mcol:
                                  E.tensor_single_scalar(o, i, m[:, 0:1], ALU.is_ge),
                                  hard_deps=[t])
                        t = mk.op("ve", lambda E, o=balt, g2=gt, i=bcur:
                                  E.scalar_tensor_tensor(
                                      o, g2, -2e9, i, ALU.mult, ALU.add), deps=[t])
                        bcur, balt = balt, bcur
                    t = mk.op("ve", lambda E, i=bcur, m=mcol: E.reduce_max(m, i, AX.X),
                              deps=[t])
                    t = mk.op("ve", lambda E, o=sel, i=b0, m=mcol:
                              E.tensor_single_scalar(o, i, m[:, 0:1], ALU.is_ge),
                              hard_deps=[t])
                    t = mk.op("ve", lambda E, o=wv, i=sel, s=scores: E.tensor_mul(
                        o, i, s), deps=[t])
                    t = mk.op("ve", lambda E, i=wv, w=wsc: E.reduce_sum(w, i, AX.X),
                              deps=[t])
                    t = mk.op("ve", lambda E, o=wic, i=wsc: E.reciprocal(o, i),
                              deps=[t])
                    combtok = mk.op("ve", lambda E, o=cwork[:, sl], i=wv, w=wic:
                                    E.tensor_scalar(o, i, w[:, 0:1], 2.5,
                                                    ALU.mult, ALU.mult),
                                    hard_deps=[t])
                for tc in range(4):
                    mk.banktok[tc] = combtok
                if dbg_st:
                    for tc in range(4):
                        mk.op("ve", lambda E, tc=tc: E.tensor_copy(
                            mrb[:, tc * 16:tc * 16 + 16], psum[tc][:, 0:16]),
                            deps=[ltoks[tc]])
                    tlg = mk.op("ve", lambda E: E.tensor_copy(
                        mrb[:, 64:128], mrb[:, 0:64]))
                    mk.dma("gp", dbg["d_lg"], mrb[:, 0:64], deps=[tlg])
                    mk.dma("gp", dbg["d_xn"], xnbuf[:], deps=[xntok])
                    mk.dma("gp", dbg["d_cw"], cwork[:], deps=[combtok])
                tct = None
                for tc in range(4):
                    ttr = mk.op("te", lambda E, tc=tc: E.transpose(
                        psum[4 + (tc % 2)][0:16, 0:128],
                        cwork[:, tc * 16:tc * 16 + 16], identb[:]),
                        deps=[combtok, mk.banktok[4 + (tc % 2)]])
                    tct = mk.op("ve", lambda E, tc=tc: E.tensor_copy(
                        statb[0:16, tc * 128:tc * 128 + 128],
                        psum[4 + (tc % 2)][0:16, 0:128]), deps=[ttr])
                    mk.banktok[4 + (tc % 2)] = tct
                mrb_rd = xntok
                w13src = w13t.rearrange("e (k p) m -> e p k m", p=128)
                w2src = w2t.rearrange("e (k p) m -> e p k m", p=128)
                for e in range(NE + 1):
                    bi = e % 2
                    mk.dma("sy",
                           w13v[bi].rearrange("p (k m) -> p k m", k=8),
                           w13src[e:e + 1], deps=[wfree[bi]], stream=f"w{bi}")
                    t2l = mk.dma("sy",
                                 w2v[bi].rearrange("p (k m) -> p k m", k=4),
                                 w2src[e:e + 1], stream=f"w{bi}")
                    if e < NE:
                        tcf = mk.dma("gp", cbuf[0:1, 0:512], statb[e:e + 1, 0:512],
                                     deps=[tct, cb_dma_ok], stream="cf")
                        cbmm = mk.op("te", lambda E: E.matmul(
                            psum[0][:], onesb[0:1, 0:128], cbuf[0:1, 0:512],
                            start=True, stop=True),
                            deps=[tcf, mk.banktok[0]])
                        cb_dma_ok = cbmm
                        cbcopy = mk.op("ve", lambda E: E.tensor_copy(
                            mrb[:], psum[0][:]), deps=[cbmm, mrb_rd])
                        mk.banktok[0] = cbcopy
                    fin = [None] * 8
                    for k in range(8):
                        sl = k % 2
                        wsrc = w13v[bi][:, k * 1024:(k + 1) * 1024]
                        wh = wrh[:, sl * 1024:(sl + 1) * 1024]
                        wl_ = wrl[:, sl * 1024:(sl + 1) * 1024]
                        xsrc = xnbuf[:, k * 512:(k + 1) * 512]
                        xh_ = xrh[:, sl * 512:(sl + 1) * 512]
                        xl_ = xrl[:, sl * 512:(sl + 1) * 512]
                        th = mk.op("se", lambda E, o=wh, i=wsrc: E.activation(
                            o, i, AF.Copy), deps=[t2l, slotmm[sl]])
                        tl = mk.op("ve", lambda E, o=wl_, i=wsrc, h=wh:
                                   E.tensor_sub(o, i, h), deps=[th])
                        txh = mk.op("se", lambda E, o=xh_, i=xsrc: E.activation(
                            o, i, AF.Copy), deps=[xntok, slotmm[sl]])
                        txl = mk.op("ve", lambda E, o=xl_, i=xsrc, h=xh_:
                                    E.tensor_sub(o, i, h), deps=[txh])
                        morder = (0, 4, 1, 5, 2, 6, 3, 7) if k == 7 else range(8)
                        last = None
                        for m in morder:
                            pb = psum[m]
                            for p in range(3):
                                lh = wh if p < 2 else wl_
                                rh = xh_ if p != 1 else xl_
                                dd = []
                                if p == 0:
                                    dd = [th, tl, txh, txl]
                                    if k == 0:
                                        dd.append(mk.banktok[m])
                                do_inc = (p == 2 and (k == 7 or m == 7))
                                last = mk.op("te", lambda E, o=pb,
                                             l=lh[:, m * 128:(m + 1) * 128],
                                             r=rh, k=k, p=p: E.matmul(
                                                 o[:], l, r,
                                                 start=(k == 0 and p == 0),
                                                 stop=(k == 7 and p == 2),
                                                 skip_group_check=True),
                                             deps=dd, inc=do_inc)
                                if do_inc and k == 7:
                                    fin[m] = last
                        slotmm[sl] = fin[7] if k == 7 else last
                    hsp = [None] * 4
                    hsph = [None] * 4
                    for pi in range(4):
                        hc = hbuf[:, pi * 512:pi * 512 + 512]
                        hh_ = hhb[:, pi * 512:(pi + 1) * 512]
                        hl_ = hlb[:, pi * 512:(pi + 1) * 512]
                        tsl = mk.op("se", lambda E, o=hc, i=psum[pi]:
                                    E.activation(o, i[:], AF.Silu),
                                    deps=[fin[pi], hbuf_free[pi]])
                        mk.banktok[pi] = tsl
                        tmu = mk.op("ve", lambda E, o=hc, i=psum[pi + 4]:
                                    E.tensor_mul(o, o, i[:]),
                                    deps=[tsl, fin[pi + 4]])
                        mk.banktok[pi + 4] = tmu
                        if e < NE:
                            tmu = mk.op("ve", lambda E, o=hc: E.tensor_mul(
                                o, o, mrb[:]), deps=[tmu])
                            mrb_rd = tmu
                        thh = mk.op("se", lambda E, o=hh_, i=hc: E.activation(
                            o, i, AF.Copy), deps=[tmu, hfree[pi]])
                        thl = mk.op("ve", lambda E, o=hl_, i=hc, h=hh_:
                                    E.tensor_sub(o, i, h), deps=[thh])
                        hsp[pi] = thl
                        hsph[pi] = thh
                        hbuf_free[pi] = thl
                    if dbg_st and e == 0:
                        mk.dma("gp", dbg["d_h"], hbuf[:], deps=[hsp[3]])
                    w2fin = [None] * 8
                    tl2 = None
                    for k in range(4):
                        sl = k % 2
                        src2 = w2v[bi][:, k * 1024:(k + 1) * 1024]
                        wh2 = wrh[:, sl * 1024:(sl + 1) * 1024]
                        wl2 = wrl[:, sl * 1024:(sl + 1) * 1024]
                        th2 = mk.op("se", lambda E, o=wh2, i=src2: E.activation(
                            o, i, AF.Copy), deps=[t2l, slotmm[sl]])
                        tl2 = mk.op("ve", lambda E, o=wl2, i=src2, h=wh2:
                                    E.tensor_sub(o, i, h), deps=[th2])
                        hh_k = hhb[:, k * 512:(k + 1) * 512]
                        hl_k = hlb[:, k * 512:(k + 1) * 512]
                        last = None
                        for m in range(8):
                            pb = psum[m]
                            for p in range(3):
                                lh = wh2 if p < 2 else wl2
                                rh = hh_k if p != 1 else hl_k
                                dd = []
                                if p == 0:
                                    dd = [th2, tl2, hsph[k], hsp[k]]
                                    if k == 0:
                                        dd.append(mk.banktok[m])
                                do_inc = (p == 2 and (k == 3 or m == 7))
                                last = mk.op("te", lambda E, o=pb,
                                             l=lh[:, m * 128:(m + 1) * 128],
                                             r=rh, k=k, p=p: E.matmul(
                                                 o[:], l, r,
                                                 start=(k == 0 and p == 0),
                                                 stop=(k == 3 and p == 2),
                                                 skip_group_check=True),
                                             deps=dd, inc=do_inc)
                                if do_inc and k == 3:
                                    w2fin[m] = last
                        slotmm[sl] = w2fin[7] if k == 3 else last
                        hfree[k] = w2fin[7] if k == 3 else last
                    wfree[bi] = tl2
                    for m in range(8):
                        tacc = mk.op("ve", lambda E, m=m, i=psum[m]: E.tensor_add(
                            xbuf[:, m * 512:m * 512 + 512],
                            xbuf[:, m * 512:m * 512 + 512], i[:]),
                            deps=[w2fin[m]])
                        mk.banktok[m] = tacc
                tst = mk.dma("gp", dst_fn(st),
                             xbuf[:].rearrange("p (k t) -> p k t", k=8),
                             deps=[mk.banktok[m] for m in range(8)])
                mk.barrier()

        def snap(name, src):
            if debug:
                mk.dma("gp", dbg[name], src)
                mk.barrier()

        qkv_pass(wqkv_a, load_x_from(x_fm), range(8),
                 [(m, q_scr) for m in range(8)] + [(m, k_scr) for m in range(8, 16)],
                 do_v=True, rope_pos=None)

        def scramble_k():
            minp = wbuf[:, 0:4096]
            ev = [wbuf[:, 4096:4608], wbuf[:, 4608:5120]]
            evtok = [None, None]
            prev_tr = None
            for h in range(16):
                hrow = (h // 2) * 128 + (h % 2) * 64
                ldt = None
                for b8 in range(8):
                    src = k_scr[hrow + b8 * 8:hrow + b8 * 8 + 8, :].rearrange(
                        "a (b c) -> b a c", c=64)
                    dst = minp[0:64, b8 * 512:b8 * 512 + 512].rearrange(
                        "p (a c) -> p a c", c=64)
                    ldt = mk.dma("sy", dst, src,
                                 deps=[prev_tr] if b8 == 0 else [], stream="x0")
                ttr = None
                for b8 in range(8):
                    pb_i = b8 % 2
                    pb = psum[pb_i]
                    for a in range(8):
                        ttr = mk.op("te", lambda E, o=pb, b8=b8, a=a: E.transpose(
                            o[0:64, a * 64:a * 64 + 64],
                            minp[0:64, (b8 * 8 + a) * 64:(b8 * 8 + a) * 64 + 64],
                            identb[0:64, 0:64]),
                            deps=([ldt, mk.banktok[pb_i]] if a == 0 else []),
                            inc=(a == 7))
                    tev = mk.op("ve", lambda E, o=ev[pb_i], i=pb: E.tensor_copy(
                        o[0:64, :], i[0:64, :]), deps=[ttr, evtok[pb_i]])
                    mk.banktok[pb_i] = tev
                    evtok[pb_i] = mk.dma(
                        "gp", k_scr[hrow:hrow + 64, b8 * 512:b8 * 512 + 512],
                        ev[pb_i][0:64, :], deps=[tev], stream=f"e{pb_i}")
                prev_tr = ttr
            mk.barrier()

        scramble_k()
        snap("d_q", q_scr)
        snap("d_k", k_scr)
        snap("d_vt", vt_scr)
        attn_pass(8, causal=True)
        snap("d_ao", ao_scr)

        def ffn1_loader(st):
            t1 = load_x_from(x_fm)(st)
            t2 = mk.dma("sy", xnbuf[:].rearrange("p (k t) -> p k t", k=8),
                        ao_scr.rearrange("(k p) t -> p k t", p=128)
                        [:, :, st * 512:(st + 1) * 512], stream="x0")
            tok = None
            for k in range(NCH):
                tok = mk.op("ve", lambda E, k=k: E.tensor_add(
                    xbuf[:, k * 512:k * 512 + 512],
                    xbuf[:, k * 512:k * 512 + 512],
                    xnbuf[:, k * 512:k * 512 + 512]), deps=[t1, t2])
            return tok

        ffn_pass(8, ffn1_loader,
                 lambda st: x2_scr.rearrange("(k p) t -> p k t", p=128)
                 [:, :, st * 512:(st + 1) * 512], dbg_first=True)
        snap("d_x2", x2_scr)

        qkv_pass(wqkv_b, load_x_from(x2_scr), range(8),
                 [(m, k_scr) for m in range(8, 16)], do_v=True,
                 rope_pos=(posf_d, 0))
        snap("d_k2", k_scr)
        snap("d_vt2", vt_scr)
        qkv_pass(wqkv_b, load_x2my, range(4),
                 [(m, q_scr) for m in range(8)], do_v=False,
                 rope_pos=(posq_d, 0))
        snap("d_q2", q_scr)
        attn_pass(4, causal=False)
        snap("d_ao2", ao_scr)

        wt = None
        for k in range(8):
            wt = mk.dma("sy", wbuf[:, k * 1024:(k + 1) * 1024],
                        wo_d[k * 128:(k + 1) * 128, :], stream="w0")
        for tt in range(4):
            tb = load_x2my(tt)
            tao = mk.dma("sy", xnbuf[:].rearrange("p (k t) -> p k t", k=8),
                         ao_scr.rearrange("(k p) t -> p k t", p=128)
                         [:, :, tt * 512:(tt + 1) * 512], deps=[tb], stream="x0")
            for m in range(8):
                pb_i = m % 2
                pb = psum[pb_i]
                last = None
                for k in range(8):
                    lhs = wbuf[:, k * 1024 + m * 128: k * 1024 + m * 128 + 128]
                    rhs = xnbuf[:, k * 512:(k + 1) * 512]
                    last = mk.op("te", lambda E, o=pb, l=lhs, r=rhs, k=k:
                                 E.matmul(o[:], l, r, start=(k == 0), stop=(k == 7)),
                                 deps=([tao, wt, mk.banktok[pb_i]] if k == 0 else []),
                                 inc=(k == 7))
                tacc = mk.op("ve", lambda E, m=m, i=pb: E.tensor_add(
                    xbuf[:, m * 512:m * 512 + 512],
                    xbuf[:, m * 512:m * 512 + 512], i[:]), deps=[last])
                mk.banktok[pb_i] = tacc
            mk.dma("gp", x3_scr.rearrange("(k p) t -> p k t", p=128)
                   [:, :, tt * 512:(tt + 1) * 512],
                   xbuf[:].rearrange("p (k t) -> p k t", k=8),
                   deps=[mk.banktok[0], mk.banktok[1]])
            mk.barrier()
        snap("d_x3", x3_scr)

        ffn_pass(4, load_x_from(x3_scr),
                 lambda st: y_out.rearrange("(k p) t -> p k t", p=128)
                 [:, :, st * 512:(st + 1) * 512])

        mk.barrier()

        def make_runner(name):
            def body(E):
                for (e, fn, waits, incspec) in mk.ops:
                    if e != name:
                        continue
                    for (se_, v) in waits:
                        E.wait_ge(sems[se_], v)
                    if fn is None:
                        continue
                    inst = fn(E)
                    if incspec is not None:
                        sname, n = incspec
                        inst.then_inc(sems[sname], n)
            return body

        block.tensor(make_runner("te"))
        block.vector(make_runner("ve"))
        block.scalar(make_runner("se"))
        block.sync(make_runner("sy"))
        block.gpsimd(make_runner("gp"))

    return nc


_NC_CACHE = None


def kernel(**inputs):
    global _NC_CACHE
    x = np.asarray(inputs["L0_attn_a_x"], dtype=np.float32)
    wqkv_a = np.ascontiguousarray(inputs["L0_attn_a_qkv_weights"], dtype=np.float32)
    wqkv_b = np.ascontiguousarray(inputs["L0_attn_b_w_qkv"], dtype=np.float32)
    wo = np.ascontiguousarray(inputs["L0_attn_b_w_o"], dtype=np.float32)
    gate_w = np.asarray(inputs["L0_ffn_gate_w"], dtype=np.float32)
    gate_b = np.asarray(inputs["L0_ffn_gate_b"], dtype=np.float32)
    w13 = np.asarray(inputs["L0_ffn_w13"], dtype=np.float32)
    w2 = np.asarray(inputs["L0_ffn_w2"], dtype=np.float32)
    w13s = np.asarray(inputs["L0_ffn_w13_s"], dtype=np.float32)
    w2s = np.asarray(inputs["L0_ffn_w2_s"], dtype=np.float32)

    w13t = np.ascontiguousarray(
        np.concatenate([w13.transpose(0, 2, 1), w13s.T[None]], axis=0))
    w2t = np.ascontiguousarray(
        np.concatenate([w2.transpose(0, 2, 1), w2s.T[None]], axis=0))
    gwt = np.ascontiguousarray(gate_w.T)
    gbv = gate_b.reshape(1, NE)

    jj = np.arange(128)[:, None]
    qq = np.arange(512)[None, :]
    masks = np.concatenate(
        [(jj + d * 128 <= qq).astype(np.float32) for d in range(4)], axis=1)
    ident = np.eye(128, dtype=np.float32)
    prot = np.zeros((128, 128), dtype=np.float32)
    for base in (0, 64):
        for m in range(32):
            prot[base + m + 32, base + m] = -1.0
            prot[base + m, base + m + 32] = 1.0
    irep = (np.arange(128) % 32).astype(np.float32).reshape(1, 128)
    pos_full = np.arange(S, dtype=np.float32).reshape(1, S)

    if _NC_CACHE is None:
        _NC_CACHE = build_program()
    nc = _NC_CACHE

    in_maps = []
    for c in range(8):
        b, hc = c // 2, c % 2
        in_maps.append({
            "x_fm": np.ascontiguousarray(x[b].T),
            "wqkv_a": wqkv_a, "wqkv_b": wqkv_b, "wo": wo,
            "gwt": gwt, "gb": gbv, "w13t": w13t, "w2t": w2t,
            "masks": masks, "ident": ident, "prot": prot, "irep": irep,
            "pos_full": pos_full,
            "pos_q": (hc * HALF + np.arange(HALF, dtype=np.float32)).reshape(1, HALF),
            "hcv": np.full((128, 1), float(hc), dtype=np.float32),
        })

    res = run_bass_kernel_spmd(nc, in_maps, core_ids=list(range(8)))
    ys = [np.array(res.results[c]["y"], copy=True) for c in range(8)]
    out = np.empty((4, S, D), dtype=np.float32)
    for c in range(8):
        b, hc = c // 2, c % 2
        out[b, hc * HALF:(hc + 1) * HALF, :] = ys[c].T
    return out
